# version 18 (final)
# baseline (speedup 1.0000x reference)
import os
import numpy as np
from contextlib import ExitStack
import concourse.bass as bass
import concourse.mybir as mybir
from concourse.bass_utils import run_bass_kernel_spmd

F32 = mybir.dt.float32
BF16 = mybir.dt.bfloat16
I32 = mybir.dt.int32
ALU = mybir.AluOpType
ACTF = mybir.ActivationFunctionType

B, L, D = 2, 8192, 1024
DC, DI, NS, DTR = 512, 1024, 16, 32
FH = 3072
EPS = 1e-5
QINV = 1.0 / 127.0

NSLICE = 4
TO = L // NSLICE
WU = 128
TE = TO + WU
CW = 512
CWF = 256
CHUNKS = [(0, WU)] + [(WU + CW * k, WU + CW * (k + 1)) for k in range(TO // CW)]
GN = 4
NGRP = NS // GN
SERIAL_SCAN = False
SERIAL_FFN = False

ENGS = ("sync", "tensor", "scalar", "vector", "gpsimd")


class Prog:
    def __init__(self):
        self.q = {e: [] for e in ENGS}
        self.cum = {e: 0 for e in ENGS}
        self.phases = []

    def phase(self, name):
        self.phases.append((name, {e: len(self.q[e]) for e in ENGS}))

    def emit(self, eng, fn, inc=True, chan=None, k=None):
        if k is None:
            k = 16 if eng == "sync" else 1
        if inc:
            sname = chan or eng
            self.cum[sname] = self.cum.get(sname, 0) + k
            self.q[eng].append(("OP", fn, sname, k))
        else:
            self.q[eng].append(("OP", fn, None, 0))

    def wait(self, eng, src, val=None):
        v = self.cum.get(src, 0) if val is None else val
        if v > 0:
            self.q[eng].append(("WAIT", src, v, 0))

    def mark(self, eng):
        return self.cum.get(eng, 0)

    def barrier(self, exclude=("gpsimd", "gdma")):
        marks = dict(self.cum)
        for e in ENGS:
            for s, v in marks.items():
                if s != e and v > 0 and s not in exclude:
                    self.q[e].append(("WAIT", s, v, 0))


def build_nc():
    nc = bass.Bass("TRN2", target_bir_lowering=False)
    p = Prog()

    def din(name, shape, dt=F32):
        return nc.dram_tensor(name, shape, dt, kind="ExternalInput").ap()

    xT = din("xT", [D, TE])
    cvals_in = din("cvals", [128, 8])
    mask2d = din("mask2d", [128, WU])
    eye_in = din("eye_in", [128, 128])
    ln_g = din("ln_g", [128, 8])
    ln_b = din("ln_b", [128, 8])
    po_bi = din("po_bi", [128, 8])
    poWT = din("poWT", [8, 128, D])
    Wm = {}
    for l in range(2):
        Wm[l] = dict(
            inWT=din(f"inWT{l}", [4, 128, 2 * DI]),
            xpWT=din(f"xpWT{l}", [8, 128, 64]),
            dtWT=din(f"dtWT{l}", [DTR, DI]),
            moWT=din(f"moWT{l}", [8, 128, DC]),
            convw=din(f"convw{l}", [128, 8, 4]),
            convb=din(f"convb{l}", [128, 8]),
            dtb=din(f"dtb{l}", [128, 8]),
            Alog=din(f"Alog{l}", [128, 8, NS]),
            Dp=din(f"Dp{l}", [128, 8]),
            rmsw=din(f"rmsw{l}", [128, 4]),
        )
    wgT = din("wgT", [8, 128, FH], I32)
    wuT = din("wuT", [8, 128, FH], I32)
    wdT = din("wdT", [24, 128, D], I32)

    wg_s = nc.dram_tensor("wg_s", [24, 128, 8, 128], BF16, kind="Internal").ap()
    wu_s = nc.dram_tensor("wu_s", [24, 128, 8, 128], BF16, kind="Internal").ap()
    wd_s = nc.dram_tensor("wd_s", [8, 128, 24, 128], BF16, kind="Internal").ap()
    rowscr = nc.dram_tensor("rowscr", [1, 2048], F32, kind="Internal").ap()
    bcscr = nc.dram_tensor("bcscr", [32, 512], BF16, kind="Internal").ap()

    outT = nc.dram_tensor("outT", [D, TO], F32, kind="ExternalOutput").ap()

    es = ExitStack()

    def sbuf(name, shape, dt):
        return es.enter_context(nc.sbuf_tensor(name, shape, dt))

    c_sb = sbuf("c_sb", [128, 8], F32)
    CONSTS = [0.0, 1.0, -0.5, -1.0, EPS, QINV, 1.0 / 512, 1.0 / 1024]
    for _ci, _cv in enumerate(CONSTS):
        nc.const_aps.register(F32, _cv, c_sb[:, _ci:_ci + 1])
    h = sbuf("h", [128, 4, TE], BF16)
    xn = sbuf("xn", [128, 8, TE], BF16)
    S = sbuf("S", [128, NS * 8], BF16)
    ones_b = sbuf("ones_b", [128, 1], BF16)
    ones1f = sbuf("ones1f", [1, 128], F32)
    zrow_b = sbuf("zrow_b", [128, 8], BF16)
    eye_b = sbuf("eye_b", [128, 128], BF16)
    mask_b = sbuf("mask_b", [128, WU], BF16)
    lng_s = sbuf("lng_s", [128, 8], F32)
    lnb_s = sbuf("lnb_s", [128, 8], F32)
    pob_s = sbuf("pob_s", [128, 8], F32)
    cl = {}
    for l in range(2):
        cl[l] = dict(
            convw=sbuf(f"convw{l}s", [128, 8, 4], F32),
            convb=sbuf(f"convb{l}s", [128, 8], F32),
            dtb=sbuf(f"dtb{l}s", [128, 8], F32),
            As=sbuf(f"As{l}s", [128, 8, NS], F32),
            Dp=sbuf(f"Dp{l}s", [128, 8], F32),
            rmsw=sbuf(f"rmsw{l}s", [128, 4], F32),
        )
    WARN = sbuf("WARN", [128, 12288], BF16)
    inWT_b = WARN[:, 0:8192].rearrange("p (a b) -> p a b", b=2 * DI)
    moWT_b = WARN[:, 8192:12288].rearrange("p (a b) -> p a b", b=DC)
    poWT_b = WARN[:, 0:8192].rearrange("p (a b) -> p a b", b=D)
    xpWT_b = sbuf("xpWT_b", [128, 8, 64], BF16)
    dtWT_b = sbuf("dtWT_b", [DTR, DI], BF16)
    stg_f = sbuf("stg_f", [128, 512], F32)
    stg_gi = sbuf("stg_gi", [128, 512], I32)
    stg_gb = sbuf("stg_gb", [128, 512], BF16)

    hn = sbuf("hn", [128, 4, CW], BF16)
    x_ch_t = sbuf("x_ch", [128, 8, CWF], F32)
    xc_pre = sbuf("xc_pre", [128, 8, 3 + CW], BF16)
    xc = xc_pre[:, :, 3:3 + CW]
    sz = sbuf("sz", [128, 8, CW], BF16)
    delta = sbuf("delta", [128, 8, CW], BF16)
    wle = sbuf("wle", [128, 8, CW], BF16)
    yg = sbuf("yg", [128, 8, CW], BF16)
    dtbc = sbuf("dtbc", [64, CW], BF16)
    dt_b = dtbc[0:DTR]
    bc_b = dtbc[DTR:DTR + 32]
    rows = sbuf("rows", [1, 1024], F32)

    def row(k, cols):
        return rows[0:1, 512 * k:512 * k + cols]
    rs_b = sbuf("rs_b", [128, CW], F32)
    mu_b = sbuf("mu_b", [128, CW], F32)
    GSW = GN * (CW + 1)
    NSW = NS * (CW + 1)
    ARN = sbuf("arena", [128, 7 * GSW + 2 * NSW], BF16)
    dAq = [ARN[:, 0:GSW], ARN[:, GSW:2 * GSW]]
    uuq = [ARN[:, 2 * GSW:3 * GSW], ARN[:, 3 * GSW:4 * GSW]]
    hbq = [ARN[:, 4 * GSW:5 * GSW], ARN[:, 5 * GSW:6 * GSW]]
    tmq = [ARN[:, 6 * GSW:7 * GSW]] * 2
    Bb = ARN[:, 7 * GSW:7 * GSW + NSW]
    Cb = ARN[:, 7 * GSW + NSW:7 * GSW + 2 * NSW]
    hup = ARN[:, 0:24 * CW].rearrange("p (a b) -> p a b", b=CW)

    def _arnv(lo, n, b=128):
        return ARN[:, lo:lo + n].rearrange("p (a b) -> p a b", b=b)

    x_ch = x_ch_t[:]
    stg2_b = None
    _fb = 24 * CW
    wgt_q = [_arnv(_fb, 1024), _arnv(_fb + 1024, 1024)]
    wut_q = [_arnv(_fb + 2048, 1024), _arnv(_fb + 3072, 1024)]
    wdt_q = [_arnv(_fb + 4096, 3072), _arnv(_fb + 7168, 3072)]
    sg_s = ARN[:, _fb + 10240:_fb + 10240 + CW]

    pb = [es.enter_context(nc.psum_tensor(f"pb{i}", [128, 512], F32))
          for i in range(4)]
    pa = [es.enter_context(nc.psum_tensor(f"pa{i}", [128, 512], F32))
          for i in range(2)]

    sems = {}

    Sv = S[:].rearrange("p (n j) -> p n j", j=8)
    hupf = ARN[:, 0:24 * CW]

    def act(out, in_, func, bias=0.0, scale=1.0):
        p.emit("scalar", lambda e, o=out, i=in_, f=func, b=bias, s=scale:
               e.activation(out=o, in_=i, func=f, bias=b, scale=s))

    def acopy(out, in_):
        p.emit("scalar", lambda e, o=out, i=in_: e.copy(out=o, in_=i))

    def vcopy(out, in_):
        p.emit("vector", lambda e, o=out, i=in_: e.tensor_copy(out=o, in_=i))

    def tt(out, in0, in1, op=ALU.mult):
        p.emit("vector", lambda e, o=out, a=in0, b=in1, op=op:
               e.tensor_tensor(out=o, in0=a, in1=b, op=op))

    def stt(out, in0, scalar, in1, op0=ALU.mult, op1=ALU.add):
        p.emit("vector", lambda e, o=out, a=in0, s=scalar, b=in1, o0=op0, o1=op1:
               e.scalar_tensor_tensor(out=o, in0=a, scalar=s, in1=b,
                                      op0=o0, op1=o1))

    def ts1(out, in0, s1, op=ALU.mult):
        p.emit("vector", lambda e, o=out, a=in0, s=s1, op=op:
               e.tensor_single_scalar(out=o, in_=a, scalar=s, op=op))

    def ts2(out, in0, s1, s2, op0=ALU.mult, op1=ALU.add):
        p.emit("vector", lambda e, o=out, a=in0, sa=s1, sb=s2, o0=op0, o1=op1:
               e.tensor_scalar(out=o, in0=a, scalar1=sa, scalar2=sb,
                               op0=o0, op1=o1))

    def dma(out, in_, chan=None):
        p.emit("sync", lambda e, o=out, i=in_: e.dma_start(out=o, in_=i),
               chan=chan)

    def memset(ap, v):
        p.emit("vector", lambda e, a=ap, v=v: e.memset(a, v))

    def jbc(row_ap, nj):
        return row_ap.unsqueeze(1).broadcast_to(
            [row_ap.shape[0], nj, row_ap.shape[-1]])

    bank_free = [None] * 4
    acc_free = [None] * 2
    _rr = [0]

    def grab_bank():
        i = _rr[0] % 4
        _rr[0] += 1
        if bank_free[i] is not None:
            p.wait("tensor", bank_free[i][0], bank_free[i][1])
        return i

    def mm(pairs, out_rows=128, row0=0, out_cols=CW):
        i = grab_bank()
        n = len(pairs)
        for k, (lt, rh) in enumerate(pairs):
            p.emit("tensor", lambda e, lt=lt, rh=rh, st=(k == 0), sp=(k == n - 1),
                   ps=pb[i][row0:row0 + out_rows, 0:out_cols]:
                   e.matmul(ps, lt, rh, start=st, stop=sp),
                   inc=(k == n - 1))
        return i

    def free_bank(i, eng):
        bank_free[i] = (eng, p.mark(eng))

    def rsqrt_row(src_ap, scale, dst, cols):
        act(row(dst, cols), src_ap, ACTF.Ln, bias=EPS, scale=scale)
        act(row(dst, cols), row(dst, cols), ACTF.Exp, scale=-0.5)

    def pe_bcast_row(ridx, dst, cols):
        i = grab_bank()
        p.wait("tensor", "vector")
        p.wait("tensor", "scalar")
        p.emit("tensor", lambda e, i=i, r=ridx, c=cols: e.matmul(
            pb[i][:, 0:c], ones1f[:], row(r, c), start=True, stop=True))
        p.wait("vector", "tensor")
        vcopy(dst[:, 0:cols], pb[i][:, 0:cols])
        free_bank(i, "vector")

    def ln_stats(get_tile, cols, nd=1024):
        nj = nd // 128
        for j in range(nj):
            act(delta[:, j, 0:cols], get_tile(j), ACTF.Square)
        p.wait("tensor", "scalar")
        p.wait("tensor", "vector")
        i = grab_bank()
        for j in range(nj):
            p.emit("tensor", lambda e, j=j, i=i, c=cols: e.matmul(
                pb[i][0:1, 0:c], ones_b[:], get_tile(j),
                start=(j == 0), stop=(j == nj - 1)), inc=(j == nj - 1))
        for j in range(nj):
            p.emit("tensor", lambda e, j=j, i=i, c=cols: e.matmul(
                pb[i][32:33, 0:c], ones_b[:], delta[:, j, 0:cols],
                start=(j == 0), stop=(j == nj - 1)), inc=(j == nj - 1))
        p.wait("vector", "tensor")
        ts1(row(0, cols), pb[i][0:1, 0:cols], 1.0 / nd)
        ts1(row(1, cols), pb[i][32:33, 0:cols], 1.0 / nd)
        tt(pb[i][32:33, 0:cols], row(0, cols), row(0, cols))
        tt(pb[i][32:33, 0:cols], row(1, cols), pb[i][32:33, 0:cols],
           op=ALU.subtract)
        p.wait("scalar", "vector")
        rsqrt_row(pb[i][32:33, 0:cols], 1.0, 1, cols)
        free_bank(i, "scalar")
        p.wait("vector", "scalar")
        tt(row(0, cols), row(0, cols), row(1, cols))
        pe_bcast_row(1, rs_b, cols)
        pe_bcast_row(0, mu_b, cols)

    def loadcast(dram_ap, stg_view, dst_ap):
        p.wait("sync", "vector")
        dma(stg_view, dram_ap)
        p.wait("vector", "sync")
        vcopy(dst_ap, stg_view)

    def load_layer_weights(l):
        w = Wm[l]
        for k in range(4):
            for hf in range(4):
                loadcast(w["inWT"][k][:, 512 * hf:512 * (hf + 1)],
                         stg_f[:, 0:512],
                         inWT_b[:, k, 512 * hf:512 * (hf + 1)])
        for k in range(8):
            loadcast(w["xpWT"][k], stg_f[:, 0:64], xpWT_b[:, k, :])
        for hf in range(2):
            loadcast(w["dtWT"][:, 512 * hf:512 * (hf + 1)],
                     stg_f[0:DTR, 0:512], dtWT_b[:, 512 * hf:512 * (hf + 1)])
        for k in range(8):
            loadcast(w["moWT"][k], stg_f[:, 0:DC], moWT_b[:, k, :])

    p.phase("P0w")
    memset(ones_b[:], 1.0)
    memset(ones1f[:], 1.0)
    memset(zrow_b[:], 0.0)
    dma(c_sb[:], cvals_in)
    dma(lng_s[:], ln_g)
    dma(lnb_s[:], ln_b)
    dma(pob_s[:], po_bi)
    for l in range(2):
        dma(cl[l]["convw"][:], Wm[l]["convw"])
        dma(cl[l]["convb"][:], Wm[l]["convb"])
        dma(cl[l]["dtb"][:], Wm[l]["dtb"])
        dma(cl[l]["Dp"][:], Wm[l]["Dp"])
        dma(cl[l]["rmsw"][:], Wm[l]["rmsw"])
        dma(cl[l]["As"][:], Wm[l]["Alog"])
    dma(stg_f[:, 0:128], eye_in)
    dma(stg_f[:, 128:128 + WU], mask2d)
    p.wait("vector", "sync")
    p.wait("scalar", "sync")
    vcopy(eye_b[:], stg_f[:, 0:128])
    vcopy(mask_b[:], stg_f[:, 128:128 + WU])
    for l in range(2):
        act(cl[l]["As"][:], cl[l]["As"][:], ACTF.Exp)
    p.wait("vector", "scalar")
    for l in range(2):
        ts1(cl[l]["As"][:], cl[l]["As"][:], -1.0)

    pieces = []
    for (wsrc, scr) in ((wgT, wg_s), (wuT, wu_s)):
        for k in range(8):
            for part in range(6):
                pieces.append((wsrc[k][:, 512 * part:512 * part + 512],
                               scr[4 * part:4 * part + 4, :, k, :]))
    for kk in range(24):
        for part in range(2):
            pieces.append((wdT[kk][:, 512 * part:512 * part + 512],
                           wd_s[4 * part:4 * part + 4, :, kk, :]))
    piece_state = {"fetch": 0, "fin": 0}
    _pf = {"fm": {}, "cm": {}}
    _gi = [stg_gi[:, 0:512], stg_f[:, 0:512].bitcast(I32)]
    _xff = x_ch_t[:].rearrange("p a b -> p (a b)")
    _xf = _xff[:, 0:512]
    _gbq = [_xff[:, 512:768].bitcast(BF16), _xff[:, 768:1024].bitcast(BF16)]

    def emit_piece():
        s = piece_state["fin"]
        lead = piece_state["fetch"] - s
        if lead >= 2 or (piece_state["fetch"] >= len(pieces) and lead >= 1):
            piece_state["fin"] = s + 1
            gb = _gbq[s % 2]
            p.wait("vector", "sync", _pf["fm"][s])
            vcopy(_xf, _gi[s % 2])
            vcopy(gb, _xf)
            _pf["cm"][s] = p.mark("vector")
            p.wait("sync", "vector", _pf["cm"][s])
            dma(pieces[s][1].rearrange("o p c -> p o c"),
                gb.rearrange("p (o c) -> p o c", c=128))
        s = piece_state["fetch"]
        if s >= len(pieces):
            return
        piece_state["fetch"] = s + 1
        if s >= 2:
            p.wait("sync", "vector", _pf["cm"][s - 2])
        dma(_gi[s % 2], pieces[s][0])
        p.wait("sync", "sync")
        _pf["fm"][s] = p.mark("sync")

    load_layer_weights(0)

    p.phase("LN1")
    LN1CH = [(0, WU)] + [(WU + CWF * k, WU + CWF * (k + 1))
                         for k in range(TO // CWF)]
    for (a, b) in LN1CH:
        cw = b - a
        p.barrier()
        dma(x_ch[:, :, 0:cw], xT.rearrange("(j q) t -> q j t", q=128)[:, :, a:b])
        p.wait("vector", "sync")
        vcopy(wle[:, :, 0:cw], x_ch[:, :, 0:cw])
        p.wait("scalar", "vector")
        ln_stats(lambda j, cw=cw: wle[:, j, 0:cw], cw)
        p.wait("vector", "sync")
        tt(x_ch[:, :, 0:cw], x_ch[:, :, 0:cw], jbc(rs_b[:, 0:cw], 8))
        tt(xn[:, :, a:b], x_ch[:, :, 0:cw], jbc(mu_b[:, 0:cw], 8),
           op=ALU.subtract)

    def ln2_affine(c_a, c_b, jlist, out_tile):
        cw = c_b - c_a
        p.wait("scalar", "vector")
        ln_stats(lambda j: xn[:, j, c_a:c_b], cw)
        p.wait("vector", "sync")
        nj = len(jlist)
        j0 = jlist[0]
        tt(yg[:, 0:nj, 0:cw], xn[:, j0:j0 + nj, c_a:c_b],
           jbc(rs_b[:, 0:cw], nj))
        tt(yg[:, 0:nj, 0:cw], yg[:, 0:nj, 0:cw], jbc(mu_b[:, 0:cw], nj),
           op=ALU.subtract)
        for idx, j in enumerate(jlist):
            ts2(out_tile(idx, j), yg[:, idx, 0:cw],
                lng_s[:, j:j + 1], lnb_s[:, j:j + 1])

    def mamba_chunk(l, ci):
        prof = (l == 1 and ci == 4)

        def sp(nm):
            if prof:
                p.phase(nm)
        a, b = CHUNKS[ci]
        cw = b - a
        swc = cw + 1
        w = cl[l]
        def gview(buf):
            return buf[:, 0:GN * swc].rearrange("p (g s) -> p g s", s=swc)

        daFq = [gview(dAq[0]), gview(dAq[1])]
        uuFq = [gview(uuq[0]), gview(uuq[1])]
        hbFq = [gview(hbq[0]), gview(hbq[1])]
        tmFq = [gview(tmq[0]), gview(tmq[1])]
        BbF = Bb[:, 0:NS * swc].rearrange("p (g s) -> p g s", s=swc)
        CbF = Cb[:, 0:NS * swc].rearrange("p (g s) -> p g s", s=swc)

        sp("m_rms")
        for j in range(4):
            tt(delta[:, j, 0:cw], h[:, j, a:b], h[:, j, a:b])
        p.wait("tensor", "vector")
        i = grab_bank()
        for j in range(4):
            p.emit("tensor", lambda e, j=j, i=i: e.matmul(
                pb[i][0:1, 0:cw], ones_b[:], delta[:, j, 0:cw],
                start=(j == 0), stop=(j == 3)), inc=(j == 3))
        p.wait("scalar", "tensor")
        rsqrt_row(pb[i][0:1, 0:cw], 1.0 / DC, 1, cw)
        free_bank(i, "scalar")
        pe_bcast_row(1, rs_b, cw)
        for j in range(4):
            stt(hn[:, j, 0:cw], h[:, j, a:b], w["rmsw"][:, j:j + 1],
                rs_b[:, 0:cw], op0=ALU.mult, op1=ALU.mult)

        sp("m_inproj")
        p.wait("tensor", "vector")
        for og in list(range(8, 16)) + list(range(8)):
            i = mm([(inWT_b[:, k, 128 * og:128 * (og + 1)], hn[:, k, 0:cw])
                    for k in range(4)], out_cols=cw)
            p.wait("scalar", "tensor")
            if og >= 8:
                act(sz[:, og - 8, 0:cw], pb[i][:, 0:cw], ACTF.Silu)
            else:
                acopy(xc_pre[:, og, 3:3 + cw], pb[i][:, 0:cw])
            free_bank(i, "scalar")
            if l == 0 and og % 2 == 0:
                emit_piece()

        if ci == 0:
            p.wait("vector", "scalar")
            tt(xc_pre[:, :, 3:3 + WU], xc_pre[:, :, 3:3 + WU],
               jbc(mask_b[:, 0:WU], 8))

        sp("m_conv")
        p.wait("vector", "scalar")
        for j in range(8):
            ts2(yg[:, j, 0:cw], xc_pre[:, j, 0:cw], w["convw"][:, j, 0:1],
                w["convb"][:, j:j + 1], op0=ALU.mult, op1=ALU.add)
            for k in range(1, 4):
                stt(yg[:, j, 0:cw], xc_pre[:, j, k:k + cw],
                    w["convw"][:, j, k:k + 1], yg[:, j, 0:cw])
        p.wait("scalar", "vector")
        acopy(xc_pre[:, :, 0:3], xc_pre[:, :, cw:cw + 3])
        act(xc_pre[:, :, 3:3 + cw], yg[:, :, 0:cw], ACTF.Silu)

        sp("m_xp")
        p.wait("tensor", "scalar")
        i = mm([(xpWT_b[:, j, :], xc[:, j, 0:cw]) for j in range(8)],
               out_rows=64, out_cols=cw)
        p.wait("scalar", "tensor")
        acopy(dtbc[0:64, 0:cw], pb[i][0:64, 0:cw])
        free_bank(i, "scalar")

        p.wait("sync", "scalar")
        dma(bcscr[:, 0:cw], bc_b[:, 0:cw])
        p.wait("sync", "sync")
        for g_ in range(NGRP):
            n0 = g_ * GN
            dma(BbF[:, n0:n0 + GN, 1:swc],
                bcscr[n0:n0 + GN, 0:cw].unsqueeze(0)
                .broadcast_to([128, GN, cw]))
            dma(CbF[:, n0:n0 + GN, 1:swc],
                bcscr[16 + n0:16 + n0 + GN, 0:cw].unsqueeze(0)
                .broadcast_to([128, GN, cw]))
        p.wait("sync", "sync")
        bcast_mark = p.mark("sync")

        sp("m_dt")
        p.wait("tensor", "scalar")
        for og in range(8):
            i = mm([(dtWT_b[0:DTR, 128 * og:128 * (og + 1)], dt_b[0:DTR, 0:cw])],
                   out_cols=cw)
            p.wait("scalar", "tensor")
            act(yg[:, og, 0:cw], pb[i][:, 0:cw], ACTF.Exp,
                bias=w["dtb"][:, og:og + 1])
            act(delta[:, og, 0:cw], yg[:, og, 0:cw], ACTF.Ln, bias=1.0)
            free_bank(i, "scalar")
            if l == 0:
                emit_piece()
        if ci == 0:
            p.wait("vector", "scalar")
            tt(delta[:, :, 0:WU], delta[:, :, 0:WU], jbc(mask_b[:, 0:WU], 8))

        p.wait("vector", "scalar")
        tt(wle[:, :, 0:cw], delta[:, :, 0:cw], xc[:, :, 0:cw])
        acopy(daFq[0][:, :, 0:1], zrow_b[:, 0:GN].unsqueeze(2))
        acopy(daFq[1][:, :, 0:1], zrow_b[:, 0:GN].unsqueeze(2))

        sp("m_scan")
        p.wait("scalar", "vector")
        scan_of = {}
        save_done = {}
        acc_last = None
        pend_save = None
        it = 0

        def flush_save():
            nonlocal pend_save
            if pend_save is not None:
                sl, hv, sm, qh = pend_save
                p.wait("scalar", "vector", sm)
                acopy(sl, hv)
                save_done[qh] = p.mark("scalar")
                pend_save = None

        for j in range(8):
            for g in range(NGRP):
                q = it % 2
                qh = it % 2
                daF, uuF = daFq[q], uuFq[q]
                hbF, tmF = hbFq[qh], tmFq[0]
                if q in scan_of:
                    p.wait("scalar", "vector", scan_of[q])
                for nl in range(GN):
                    n = g * GN + nl
                    act(daF[:, nl, 1:swc], delta[:, j, 0:cw], ACTF.Exp,
                        scale=w["As"][:, j, n:n + 1])
                acopy(uuF[:, :, 0:1], Sv[:, g * GN:(g + 1) * GN, j:j + 1])
                inj = p.mark("scalar")
                flush_save()
                if j == 0:
                    p.wait("vector", "sync", bcast_mark)
                tt(uuF[:, :, 1:swc], jbc(wle[:, j, 0:cw], GN),
                   BbF[:, g * GN:(g + 1) * GN, 1:swc])
                p.wait("vector", "scalar", inj)
                if qh in save_done:
                    p.wait("vector", "scalar", save_done[qh])
                p.emit("vector", lambda e, fl=GN * swc, q=q, qh=qh:
                       e.tensor_tensor_scan(
                           out=hbq[qh][:, 0:fl], data0=dAq[q][:, 0:fl],
                           data1=uuq[q][:, 0:fl], initial=0.0,
                           op0=ALU.mult, op1=ALU.add))
                scan_of[q] = p.mark("vector")
                if acc_last is not None:
                    p.wait("vector", "tensor", acc_last)
                tt(tmF[:, :, 1:swc], hbF[:, :, 1:swc],
                   CbF[:, g * GN:(g + 1) * GN, 1:swc])
                ym = p.mark("vector")
                pend_save = (Sv[:, g * GN:(g + 1) * GN, j:j + 1],
                             hbF[:, :, swc - 1:swc], scan_of[q], qh)
                p.wait("tensor", "vector", ym)
                if g == 0 and acc_free[j % 2] is not None:
                    p.wait("tensor", acc_free[j % 2][0], acc_free[j % 2][1])
                for nl in range(GN):
                    p.emit("tensor", lambda e, nl=nl, j=j, g=g, swc=swc, cw=cw,
                           tm3=tmF: e.matmul(
                               pa[j % 2][:, 0:cw], eye_b[:], tm3[:, nl, 1:swc],
                               start=(g == 0 and nl == 0),
                               stop=(g == NGRP - 1 and nl == GN - 1)),
                           inc=(nl == GN - 1))
                acc_last = p.mark("tensor")
                it += 1
            p.wait("vector", "tensor", acc_last)
            stt(yg[:, j, 0:cw], xc[:, j, 0:cw], w["Dp"][:, j:j + 1],
                pa[j % 2][:, 0:cw])
            tt(yg[:, j, 0:cw], yg[:, j, 0:cw], sz[:, j, 0:cw])
            acc_free[j % 2] = ("vector", p.mark("vector"))
            if l == 0:
                emit_piece()
                emit_piece()
        flush_save()

        sp("m_mo")
        p.wait("tensor", "vector")
        for og in range(4):
            i = mm([(moWT_b[:, k, 128 * og:128 * (og + 1)], yg[:, k, 0:cw])
                    for k in range(8)], out_cols=cw)
            p.wait("vector", "tensor")
            tt(h[:, og, a:b], h[:, og, a:b], pb[i][:, 0:cw], op=ALU.add)
            free_bank(i, "vector")

    p.barrier()
    memset(S[:], 0.0)
    memset(xc_pre[:, :, 0:3], 0.0)
    for ci in range(len(CHUNKS)):
        p.phase(f"L0c{ci}")
        p.barrier()
        a, b = CHUNKS[ci]
        ln2_affine(a, b, [0, 1, 2, 3],
                   lambda idx, j, a=a, b=b: h[:, j, a:b])
        mamba_chunk(0, ci)

    while piece_state["fin"] < len(pieces):
        emit_piece()
    ffn_scratch_done = p.mark("sync")

    p.barrier()
    load_layer_weights(1)
    memset(S[:], 0.0)
    memset(xc_pre[:, :, 0:3], 0.0)
    for ci in range(len(CHUNKS)):
        p.phase(f"L1c{ci}")
        p.barrier()
        mamba_chunk(1, ci)
        if ci == 4:
            p.phase("m_end")

    p.barrier()
    p.wait("sync", "sync")
    for k in range(8):
        for hf in range(2):
            loadcast(poWT[k][:, 512 * hf:512 * (hf + 1)], stg_f[:, 0:512],
                     poWT_b[:, k, 512 * hf:512 * (hf + 1)])
    for ci in range(1, len(CHUNKS)):
        p.phase(f"P3c{ci}")
        p.barrier()
        a, b = CHUNKS[ci]
        cw = b - a
        oa = a - WU
        ln2_affine(a, b, list(range(8)),
                   lambda idx, j, cw=cw: xc[:, j, 0:cw])
        p.wait("vector", "vector")
        tt(hn[:, 0:4, 0:cw], xc[:, 4:8, 0:cw], h[:, :, a:b], op=ALU.add)
        p.wait("tensor", "vector")
        for og in range(8):
            pairs = [(poWT_b[:, k, 128 * og:128 * (og + 1)], xc[:, k, 0:cw])
                     for k in range(4)]
            pairs += [(poWT_b[:, k + 4, 128 * og:128 * (og + 1)],
                       hn[:, k, 0:cw]) for k in range(4)]
            i = mm(pairs, out_cols=cw)
            p.wait("vector", "tensor")
            stt(sz[:, og, 0:cw], pb[i][:, 0:cw], pob_s[:, og:og + 1],
                xc[:, og, 0:cw], op0=ALU.add, op1=ALU.add)
            free_bank(i, "vector")
        gu_sm = {}
        gu_mm = {}

        def gu_fetch(og):
            q = og % 2
            if q in gu_mm:
                p.wait("sync", "tensor", gu_mm[q])
            dma(wgt_q[q], wg_s[og])
            dma(wut_q[q], wu_s[og])
            gu_sm[q] = p.mark("sync")

        gu_fetch(0)
        for og in range(24):
            q = og % 2
            p.wait("tensor", "sync", gu_sm[q])
            ig = mm([(wgt_q[q][:, k, :], xn[:, k, a:b]) for k in range(8)],
                    out_cols=cw)
            iu = mm([(wut_q[q][:, k, :], xn[:, k, a:b]) for k in range(8)],
                    out_cols=cw)
            gu_mm[q] = p.mark("tensor")
            if og + 1 < 24:
                gu_fetch(og + 1)
            p.wait("scalar", "tensor")
            p.wait("scalar", "vector")
            act(sg_s[:, 0:cw], pb[ig][:, 0:cw], ACTF.Silu, scale=QINV)
            free_bank(ig, "scalar")
            p.wait("vector", "scalar")
            stt(hup[:, og, 0:cw], sg_s[:, 0:cw], QINV, pb[iu][:, 0:cw],
                op0=ALU.mult, op1=ALU.mult)
            free_bank(iu, "vector")
        dn_sm = {}
        dn_mm = {}

        def dn_fetch(oc):
            q = oc % 2
            if q in dn_mm:
                p.wait("sync", "tensor", dn_mm[q])
            dma(wdt_q[q], wd_s[oc])
            dn_sm[q] = p.mark("sync")

        dn_fetch(0)
        for oc in range(8):
            q = oc % 2
            p.wait("tensor", "sync", dn_sm[q])
            p.wait("tensor", "vector")
            i = mm([(wdt_q[q][:, kk, :], hup[:, kk, 0:cw]) for kk in range(24)],
                   out_cols=cw)
            dn_mm[q] = p.mark("tensor")
            if oc + 1 < 8:
                dn_fetch(oc + 1)
            p.wait("scalar", "tensor")
            act(wle[:, oc, 0:cw], pb[i][:, 0:cw], ACTF.Silu, scale=QINV)
            free_bank(i, "scalar")
        p.wait("vector", "scalar")
        tt(sz[:, :, 0:cw], sz[:, :, 0:cw], wle[:, :, 0:cw])
        for sub in range(cw // CWF):
            sa = a + CWF * sub
            p.wait("sync", "vector")
            p.wait("sync", "tensor")
            dma(x_ch[:, :, 0:CWF],
                xT.rearrange("(j q) t -> q j t", q=128)[:, :, sa:sa + CWF])
            p.wait("vector", "sync")
            tt(x_ch[:, :, 0:CWF], x_ch[:, :, 0:CWF],
               sz[:, :, CWF * sub:CWF * (sub + 1)], op=ALU.add)
            p.wait("sync", "vector")
            dma(outT.rearrange("(j q) t -> q j t", q=128)
                [:, :, oa + CWF * sub:oa + CWF * (sub + 1)],
                x_ch[:, :, 0:CWF])

    p.barrier()
    for e in ENGS:
        p.wait(e, e, p.mark(e))

    for _sn in p.cum:
        sems[_sn] = es.enter_context(nc.semaphore(f"sem_{_sn}"))
    block = es.enter_context(nc.Block())

    strict_edges = bool(os.environ.get("BASS_SIM_FP64"))

    def run_queue(engobj, name):
        done = 0
        bounds = [(nm, idx[name]) for (nm, idx) in p.phases]
        bounds.append(("end", len(p.q[name])))
        qi = 0
        for bi in range(len(bounds) - 1):
            nm, lo = bounds[bi]
            hi = bounds[bi + 1][1]
            if hi <= lo:
                continue
            with nc.named_scope(nm):
                for item in p.q[name][lo:hi]:
                    if item[0] == "WAIT":
                        engobj.wait_ge(sems[item[1]], item[2])
                    else:
                        if strict_edges and done > 0:
                            engobj.wait_ge(sems[name], done)
                        instr = item[1](engobj)
                        if item[2]:
                            instr.then_inc(sems[name], item[3])
                            done += item[3]
        assert qi == 0

    @block.sync
    def _(e):
        run_queue(e, "sync")

    @block.tensor
    def _(e):
        run_queue(e, "tensor")

    @block.scalar
    def _(e):
        run_queue(e, "scalar")

    @block.vector
    def _(e):
        run_queue(e, "vector")

    @block.gpsimd
    def _(e):
        run_queue(e, "gpsimd")

    es.close()
    return nc


_NC_CACHE = [None]


def kernel(L0_attn_src, L0_attn_ln_g, L0_attn_ln_b, L0_attn_po_W, L0_attn_po_b,
           L0_attn_rms_w, L0_attn_in_W, L0_attn_conv_w, L0_attn_conv_b,
           L0_attn_xp_W, L0_attn_dt_W, L0_attn_dt_b, L0_attn_A_log, L0_attn_Dp,
           L0_attn_mo_W, L0_ffn_w_gate, L0_ffn_w_up, L0_ffn_w_down):
    def f(x):
        return np.ascontiguousarray(np.asarray(x), dtype=np.float32)

    def ii(x):
        return np.ascontiguousarray(np.asarray(x), dtype=np.int32)

    def col128(v):
        return np.ascontiguousarray(f(v).reshape(8, 128).T)

    x = f(L0_attn_src)
    common = dict(
        cvals=np.tile(np.array([0.0, 1.0, -0.5, -1.0, EPS, QINV,
                                1.0 / 512, 1.0 / 1024], np.float32), (128, 1)),
        eye_in=np.eye(128, dtype=np.float32),
        ln_g=col128(L0_attn_ln_g),
        ln_b=col128(L0_attn_ln_b),
        po_bi=col128(L0_attn_po_b),
        poWT=np.ascontiguousarray(f(L0_attn_po_W).T.reshape(8, 128, D)),
        wgT=np.ascontiguousarray(ii(L0_ffn_w_gate).T.reshape(8, 128, FH)),
        wuT=np.ascontiguousarray(ii(L0_ffn_w_up).T.reshape(8, 128, FH)),
        wdT=np.ascontiguousarray(ii(L0_ffn_w_down).T.reshape(24, 128, D)),
    )
    for l in range(2):
        common[f"inWT{l}"] = np.ascontiguousarray(
            f(L0_attn_in_W[l]).T.reshape(4, 128, 2 * DI))
        common[f"xpWT{l}"] = np.ascontiguousarray(
            f(L0_attn_xp_W[l]).T.reshape(8, 128, 64))
        common[f"dtWT{l}"] = np.ascontiguousarray(f(L0_attn_dt_W[l]).T)
        common[f"moWT{l}"] = np.ascontiguousarray(
            f(L0_attn_mo_W[l]).T.reshape(8, 128, DC))
        common[f"convw{l}"] = np.ascontiguousarray(
            f(L0_attn_conv_w[l][:, 0, :]).reshape(8, 128, 4).transpose(1, 0, 2))
        common[f"convb{l}"] = col128(L0_attn_conv_b[l])
        common[f"dtb{l}"] = col128(L0_attn_dt_b[l])
        common[f"Alog{l}"] = np.ascontiguousarray(
            f(L0_attn_A_log[l]).reshape(8, 128, NS).transpose(1, 0, 2))
        common[f"Dp{l}"] = col128(L0_attn_Dp[l])
        common[f"rmsw{l}"] = np.ascontiguousarray(
            f(L0_attn_rms_w[l]).reshape(4, 128).T)

    in_maps = []
    for core in range(8):
        bidx, s = divmod(core, NSLICE)
        t0 = s * TO
        if s == 0:
            blk = np.concatenate(
                [np.zeros((WU, D), np.float32), x[bidx, 0:TO]], axis=0)
            mask = np.zeros((128, WU), np.float32)
        else:
            blk = x[bidx, t0 - WU:t0 + TO]
            mask = np.ones((128, WU), np.float32)
        m = dict(common)
        m["xT"] = np.ascontiguousarray(blk.T)
        m["mask2d"] = mask
        in_maps.append(m)

    if _NC_CACHE[0] is None:
        _NC_CACHE[0] = build_nc()
    res = run_bass_kernel_spmd(_NC_CACHE[0], in_maps, core_ids=list(range(8)))

    out = np.empty((B, L, D), np.float32)
    for core in range(8):
        bidx, s = divmod(core, NSLICE)
        out[bidx, s * TO:(s + 1) * TO] = res.results[core]["outT"].T
    return out


# version 20
# speedup vs baseline: 1.1681x; 1.1681x over previous
import os
import numpy as np
from contextlib import ExitStack
import concourse.bass as bass
import concourse.mybir as mybir
from concourse.bass_utils import run_bass_kernel_spmd

F32 = mybir.dt.float32
BF16 = mybir.dt.bfloat16
I32 = mybir.dt.int32
ALU = mybir.AluOpType
ACTF = mybir.ActivationFunctionType

B, L, D = 2, 8192, 1024
DC, DI, NS, DTR = 512, 1024, 16, 32
FH = 3072
EPS = 1e-5
QINV = 1.0 / 127.0

NSLICE = 4
TO = L // NSLICE
WU = 128
TE = TO + WU
CW = 512
CWF = 256
CHUNKS = [(0, WU)] + [(WU + CW * k, WU + CW * (k + 1)) for k in range(TO // CW)]
GN = 4
NGRP = NS // GN
SERIAL_SCAN = False
SERIAL_FFN = False

ENGS = ("sync", "tensor", "scalar", "vector", "gpsimd")


class Prog:
    def __init__(self):
        self.q = {e: [] for e in ENGS}
        self.cum = {e: 0 for e in ENGS}
        self.phases = []

    def phase(self, name):
        self.phases.append((name, {e: len(self.q[e]) for e in ENGS}))

    def emit(self, eng, fn, inc=True, chan=None, k=None):
        if k is None:
            k = 16 if eng == "sync" else 1
        if inc:
            sname = chan or eng
            self.cum[sname] = self.cum.get(sname, 0) + k
            self.q[eng].append(("OP", fn, sname, k))
        else:
            self.q[eng].append(("OP", fn, None, 0))

    def wait(self, eng, src, val=None):
        v = self.cum.get(src, 0) if val is None else val
        if v > 0:
            self.q[eng].append(("WAIT", src, v, 0))

    def mark(self, eng):
        return self.cum.get(eng, 0)

    def barrier(self, exclude=("gpsimd", "gdma")):
        marks = dict(self.cum)
        for e in ENGS:
            for s, v in marks.items():
                if s != e and v > 0 and s not in exclude:
                    self.q[e].append(("WAIT", s, v, 0))


def build_nc():
    nc = bass.Bass("TRN2", target_bir_lowering=False)
    p = Prog()

    def din(name, shape, dt=F32):
        return nc.dram_tensor(name, shape, dt, kind="ExternalInput").ap()

    xT = din("xT", [D, TE])
    cvals_in = din("cvals", [128, 8])
    mask2d = din("mask2d", [128, WU])
    eye_in = din("eye_in", [128, 128])
    ln_g = din("ln_g", [128, 8])
    ln_b = din("ln_b", [128, 8])
    po_bi = din("po_bi", [128, 8])
    poWT = din("poWT", [8, 128, D])
    Wm = {}
    for l in range(2):
        Wm[l] = dict(
            inWT=din(f"inWT{l}", [4, 128, 2 * DI]),
            xpWT=din(f"xpWT{l}", [8, 128, 64]),
            dtWT=din(f"dtWT{l}", [DTR, DI]),
            moWT=din(f"moWT{l}", [8, 128, DC]),
            convw=din(f"convw{l}", [128, 8, 4]),
            convb=din(f"convb{l}", [128, 8]),
            dtb=din(f"dtb{l}", [128, 8]),
            Alog=din(f"Alog{l}", [128, 8, NS]),
            Dp=din(f"Dp{l}", [128, 8]),
            rmsw=din(f"rmsw{l}", [128, 4]),
        )
    wgT = din("wgT", [8, 128, FH], I32)
    wuT = din("wuT", [8, 128, FH], I32)
    wdT = din("wdT", [24, 128, D], I32)

    wg_s = nc.dram_tensor("wg_s", [24, 128, 8, 128], BF16, kind="Internal").ap()
    wu_s = nc.dram_tensor("wu_s", [24, 128, 8, 128], BF16, kind="Internal").ap()
    wd_s = nc.dram_tensor("wd_s", [8, 128, 24, 128], BF16, kind="Internal").ap()
    rowscr = nc.dram_tensor("rowscr", [1, 2048], F32, kind="Internal").ap()
    bcscr = nc.dram_tensor("bcscr", [32, 512], BF16, kind="Internal").ap()

    outT = nc.dram_tensor("outT", [D, TO], F32, kind="ExternalOutput").ap()

    es = ExitStack()

    def sbuf(name, shape, dt):
        return es.enter_context(nc.sbuf_tensor(name, shape, dt))

    c_sb = sbuf("c_sb", [128, 8], F32)
    CONSTS = [0.0, 1.0, -0.5, -1.0, EPS, QINV, 1.0 / 512, 1.0 / 1024]
    for _ci, _cv in enumerate(CONSTS):
        nc.const_aps.register(F32, _cv, c_sb[:, _ci:_ci + 1])
    h = sbuf("h", [128, 4, TE], BF16)
    xn = sbuf("xn", [128, 8, TE], BF16)
    S = sbuf("S", [128, NS * 8], BF16)
    ones_b = sbuf("ones_b", [128, 1], BF16)
    ones1f = sbuf("ones1f", [1, 128], F32)
    zrow_b = sbuf("zrow_b", [128, 8], BF16)
    eye_b = sbuf("eye_b", [128, 128], BF16)
    mask_b = sbuf("mask_b", [128, WU], BF16)
    lng_s = sbuf("lng_s", [128, 8], F32)
    lnb_s = sbuf("lnb_s", [128, 8], F32)
    pob_s = sbuf("pob_s", [128, 8], F32)
    cl = {}
    for l in range(2):
        cl[l] = dict(
            convw=sbuf(f"convw{l}s", [128, 8, 4], F32),
            convb=sbuf(f"convb{l}s", [128, 8], F32),
            dtb=sbuf(f"dtb{l}s", [128, 8], F32),
            As=sbuf(f"As{l}s", [128, 8, NS], F32),
            Dp=sbuf(f"Dp{l}s", [128, 8], F32),
            rmsw=sbuf(f"rmsw{l}s", [128, 4], F32),
        )
    WARN = sbuf("WARN", [128, 12288], BF16)
    inWT_b = WARN[:, 0:8192].rearrange("p (a b) -> p a b", b=2 * DI)
    moWT_b = WARN[:, 8192:12288].rearrange("p (a b) -> p a b", b=DC)
    poWT_b = WARN[:, 0:8192].rearrange("p (a b) -> p a b", b=D)
    xpWT_b = sbuf("xpWT_b", [128, 8, 64], BF16)
    dtWT_b = sbuf("dtWT_b", [DTR, DI], BF16)
    stg_f = sbuf("stg_f", [128, 512], F32)
    stg_gi = sbuf("stg_gi", [128, 512], I32)
    stg_gb = sbuf("stg_gb", [128, 512], BF16)

    hn = sbuf("hn", [128, 4, CW], BF16)
    x_ch_t = sbuf("x_ch", [128, 8, CWF], F32)
    xc_pre = sbuf("xc_pre", [128, 8, 3 + CW], BF16)
    xc = xc_pre[:, :, 3:3 + CW]
    sz = sbuf("sz", [128, 8, CW], BF16)
    delta = sbuf("delta", [128, 8, CW], BF16)
    wle = sbuf("wle", [128, 8, CW], BF16)
    yg = sbuf("yg", [128, 8, CW], BF16)
    dtbc = sbuf("dtbc", [64, CW], BF16)
    dt_b = dtbc[0:DTR]
    bc_b = dtbc[DTR:DTR + 32]
    rows = sbuf("rows", [1, 1024], F32)

    def row(k, cols):
        return rows[0:1, 512 * k:512 * k + cols]
    rs_b = sbuf("rs_b", [128, CW], F32)
    mu_b = sbuf("mu_b", [128, CW], F32)
    GSW = GN * (CW + 1)
    NSW = NS * (CW + 1)
    ARN = sbuf("arena", [128, 7 * GSW + 2 * NSW], BF16)
    dAq = [ARN[:, 0:GSW], ARN[:, GSW:2 * GSW]]
    uuq = [ARN[:, 2 * GSW:3 * GSW], ARN[:, 3 * GSW:4 * GSW]]
    hbq = [ARN[:, 4 * GSW:5 * GSW], ARN[:, 5 * GSW:6 * GSW]]
    tmq = [ARN[:, 6 * GSW:7 * GSW]] * 2
    Bb = ARN[:, 7 * GSW:7 * GSW + NSW]
    Cb = ARN[:, 7 * GSW + NSW:7 * GSW + 2 * NSW]
    hup = ARN[:, 0:24 * CW].rearrange("p (a b) -> p a b", b=CW)

    def _arnv(lo, n, b=128):
        return ARN[:, lo:lo + n].rearrange("p (a b) -> p a b", b=b)

    x_ch = x_ch_t[:]
    stg2_b = None
    _fb = 24 * CW
    wgt_q = [_arnv(_fb, 1024), _arnv(_fb + 1024, 1024)]
    wut_q = [_arnv(_fb + 2048, 1024), _arnv(_fb + 3072, 1024)]
    wdt_q = [_arnv(_fb + 4096, 3072), _arnv(_fb + 7168, 3072)]
    sg_s = ARN[:, _fb + 10240:_fb + 10240 + CW]

    pb = [es.enter_context(nc.psum_tensor(f"pb{i}", [128, 512], F32))
          for i in range(4)]
    pa = [es.enter_context(nc.psum_tensor(f"pa{i}", [128, 512], F32))
          for i in range(2)]

    sems = {}

    Sv = S[:].rearrange("p (n j) -> p n j", j=8)
    hupf = ARN[:, 0:24 * CW]

    def act(out, in_, func, bias=0.0, scale=1.0):
        p.emit("scalar", lambda e, o=out, i=in_, f=func, b=bias, s=scale:
               e.activation(out=o, in_=i, func=f, bias=b, scale=s))

    def acopy(out, in_):
        p.emit("scalar", lambda e, o=out, i=in_: e.copy(out=o, in_=i))

    def vcopy(out, in_):
        p.emit("vector", lambda e, o=out, i=in_: e.tensor_copy(out=o, in_=i))

    def tt(out, in0, in1, op=ALU.mult):
        p.emit("vector", lambda e, o=out, a=in0, b=in1, op=op:
               e.tensor_tensor(out=o, in0=a, in1=b, op=op))

    def stt(out, in0, scalar, in1, op0=ALU.mult, op1=ALU.add):
        p.emit("vector", lambda e, o=out, a=in0, s=scalar, b=in1, o0=op0, o1=op1:
               e.scalar_tensor_tensor(out=o, in0=a, scalar=s, in1=b,
                                      op0=o0, op1=o1))

    def ts1(out, in0, s1, op=ALU.mult):
        p.emit("vector", lambda e, o=out, a=in0, s=s1, op=op:
               e.tensor_single_scalar(out=o, in_=a, scalar=s, op=op))

    def ts2(out, in0, s1, s2, op0=ALU.mult, op1=ALU.add):
        p.emit("vector", lambda e, o=out, a=in0, sa=s1, sb=s2, o0=op0, o1=op1:
               e.tensor_scalar(out=o, in0=a, scalar1=sa, scalar2=sb,
                               op0=o0, op1=o1))

    def dma(out, in_, chan=None):
        p.emit("sync", lambda e, o=out, i=in_: e.dma_start(out=o, in_=i),
               chan=chan)

    def memset(ap, v):
        p.emit("vector", lambda e, a=ap, v=v: e.memset(a, v))

    def jbc(row_ap, nj):
        return row_ap.unsqueeze(1).broadcast_to(
            [row_ap.shape[0], nj, row_ap.shape[-1]])

    bank_free = [None] * 4
    acc_free = [None] * 2
    _rr = [0]

    def grab_bank():
        i = _rr[0] % 4
        _rr[0] += 1
        if bank_free[i] is not None:
            p.wait("tensor", bank_free[i][0], bank_free[i][1])
        return i

    def mm(pairs, out_rows=128, row0=0, out_cols=CW):
        i = grab_bank()
        n = len(pairs)
        for k, (lt, rh) in enumerate(pairs):
            p.emit("tensor", lambda e, lt=lt, rh=rh, st=(k == 0), sp=(k == n - 1),
                   ps=pb[i][row0:row0 + out_rows, 0:out_cols]:
                   e.matmul(ps, lt, rh, start=st, stop=sp),
                   inc=(k == n - 1))
        return i

    def free_bank(i, eng):
        bank_free[i] = (eng, p.mark(eng))

    def rsqrt_row(src_ap, scale, dst, cols):
        act(row(dst, cols), src_ap, ACTF.Ln, bias=EPS, scale=scale)
        act(row(dst, cols), row(dst, cols), ACTF.Exp, scale=-0.5)

    def pe_bcast_row(ridx, dst, cols):
        i = grab_bank()
        p.wait("tensor", "vector")
        p.wait("tensor", "scalar")
        p.emit("tensor", lambda e, i=i, r=ridx, c=cols: e.matmul(
            pb[i][:, 0:c], ones1f[:], row(r, c), start=True, stop=True))
        p.wait("vector", "tensor")
        vcopy(dst[:, 0:cols], pb[i][:, 0:cols])
        free_bank(i, "vector")

    def ln_stats(get_tile, cols, nd=1024):
        nj = nd // 128
        for j in range(nj):
            act(delta[:, j, 0:cols], get_tile(j), ACTF.Square)
        p.wait("tensor", "scalar")
        p.wait("tensor", "vector")
        i = grab_bank()
        for j in range(nj):
            p.emit("tensor", lambda e, j=j, i=i, c=cols: e.matmul(
                pb[i][0:1, 0:c], ones_b[:], get_tile(j),
                start=(j == 0), stop=(j == nj - 1)), inc=(j == nj - 1))
        for j in range(nj):
            p.emit("tensor", lambda e, j=j, i=i, c=cols: e.matmul(
                pb[i][32:33, 0:c], ones_b[:], delta[:, j, 0:cols],
                start=(j == 0), stop=(j == nj - 1)), inc=(j == nj - 1))
        p.wait("vector", "tensor")
        ts1(row(0, cols), pb[i][0:1, 0:cols], 1.0 / nd)
        ts1(row(1, cols), pb[i][32:33, 0:cols], 1.0 / nd)
        tt(pb[i][32:33, 0:cols], row(0, cols), row(0, cols))
        tt(pb[i][32:33, 0:cols], row(1, cols), pb[i][32:33, 0:cols],
           op=ALU.subtract)
        p.wait("scalar", "vector")
        rsqrt_row(pb[i][32:33, 0:cols], 1.0, 1, cols)
        free_bank(i, "scalar")
        p.wait("vector", "scalar")
        tt(row(0, cols), row(0, cols), row(1, cols))
        pe_bcast_row(1, rs_b, cols)
        pe_bcast_row(0, mu_b, cols)

    def loadcast(dram_ap, stg_view, dst_ap):
        p.wait("sync", "vector")
        dma(stg_view, dram_ap)
        p.wait("vector", "sync")
        vcopy(dst_ap, stg_view)

    def load_layer_weights(l):
        w = Wm[l]
        for k in range(4):
            for hf in range(4):
                loadcast(w["inWT"][k][:, 512 * hf:512 * (hf + 1)],
                         stg_f[:, 0:512],
                         inWT_b[:, k, 512 * hf:512 * (hf + 1)])
        for k in range(8):
            loadcast(w["xpWT"][k], stg_f[:, 0:64], xpWT_b[:, k, :])
        for hf in range(2):
            loadcast(w["dtWT"][:, 512 * hf:512 * (hf + 1)],
                     stg_f[0:DTR, 0:512], dtWT_b[:, 512 * hf:512 * (hf + 1)])
        for k in range(8):
            loadcast(w["moWT"][k], stg_f[:, 0:DC], moWT_b[:, k, :])

    p.phase("P0w")
    memset(ones_b[:], 1.0)
    memset(ones1f[:], 1.0)
    memset(zrow_b[:], 0.0)
    dma(c_sb[:], cvals_in)
    dma(lng_s[:], ln_g)
    dma(lnb_s[:], ln_b)
    dma(pob_s[:], po_bi)
    for l in range(2):
        dma(cl[l]["convw"][:], Wm[l]["convw"])
        dma(cl[l]["convb"][:], Wm[l]["convb"])
        dma(cl[l]["dtb"][:], Wm[l]["dtb"])
        dma(cl[l]["Dp"][:], Wm[l]["Dp"])
        dma(cl[l]["rmsw"][:], Wm[l]["rmsw"])
        dma(cl[l]["As"][:], Wm[l]["Alog"])
    dma(stg_f[:, 0:128], eye_in)
    dma(stg_f[:, 128:128 + WU], mask2d)
    p.wait("vector", "sync")
    p.wait("scalar", "sync")
    vcopy(eye_b[:], stg_f[:, 0:128])
    vcopy(mask_b[:], stg_f[:, 128:128 + WU])
    for l in range(2):
        act(cl[l]["As"][:], cl[l]["As"][:], ACTF.Exp)
    p.wait("vector", "scalar")
    for l in range(2):
        ts1(cl[l]["As"][:], cl[l]["As"][:], -1.0)

    pieces = []
    for (wsrc, scr) in ((wgT, wg_s), (wuT, wu_s)):
        for k in range(8):
            for part in range(6):
                pieces.append((wsrc[k][:, 512 * part:512 * part + 512],
                               scr[4 * part:4 * part + 4, :, k, :]))
    for kk in range(24):
        for part in range(2):
            pieces.append((wdT[kk][:, 512 * part:512 * part + 512],
                           wd_s[4 * part:4 * part + 4, :, kk, :]))
    piece_state = {"fetch": 0, "fin": 0}
    _pf = {"fm": {}, "cm": {}}
    _gi = [stg_gi[:, 0:512], stg_f[:, 0:512].bitcast(I32)]
    _xff = x_ch_t[:].rearrange("p a b -> p (a b)")
    _xf = _xff[:, 0:512]
    _gbq = [_xff[:, 512:768].bitcast(BF16), _xff[:, 768:1024].bitcast(BF16)]

    def emit_piece():
        s = piece_state["fin"]
        lead = piece_state["fetch"] - s
        if lead >= 2 or (piece_state["fetch"] >= len(pieces) and lead >= 1):
            piece_state["fin"] = s + 1
            gb = _gbq[s % 2]
            p.wait("vector", "sync", _pf["fm"][s])
            vcopy(_xf, _gi[s % 2])
            vcopy(gb, _xf)
            _pf["cm"][s] = p.mark("vector")
            p.wait("sync", "vector", _pf["cm"][s])
            dma(pieces[s][1].rearrange("o p c -> p o c"),
                gb.rearrange("p (o c) -> p o c", c=128))
        s = piece_state["fetch"]
        if s >= len(pieces):
            return
        piece_state["fetch"] = s + 1
        if s >= 2:
            p.wait("sync", "vector", _pf["cm"][s - 2])
        dma(_gi[s % 2], pieces[s][0])
        p.wait("sync", "sync")
        _pf["fm"][s] = p.mark("sync")

    load_layer_weights(0)

    p.phase("LN1")
    LN1CH = [(0, WU)] + [(WU + CWF * k, WU + CWF * (k + 1))
                         for k in range(TO // CWF)]
    for (a, b) in LN1CH:
        cw = b - a
        p.wait("sync", "vector")
        dma(x_ch[:, :, 0:cw], xT.rearrange("(j q) t -> q j t", q=128)[:, :, a:b])
        p.wait("vector", "sync")
        p.wait("vector", "scalar")
        p.wait("vector", "tensor")
        vcopy(wle[:, :, 0:cw], x_ch[:, :, 0:cw])
        p.wait("scalar", "vector")
        p.wait("scalar", "tensor")
        ln_stats(lambda j, cw=cw: wle[:, j, 0:cw], cw)
        p.wait("vector", "sync")
        tt(x_ch[:, :, 0:cw], x_ch[:, :, 0:cw], jbc(rs_b[:, 0:cw], 8))
        tt(xn[:, :, a:b], x_ch[:, :, 0:cw], jbc(mu_b[:, 0:cw], 8),
           op=ALU.subtract)

    def ln2_affine(c_a, c_b, jlist, out_tile):
        cw = c_b - c_a
        p.wait("scalar", "vector")
        ln_stats(lambda j: xn[:, j, c_a:c_b], cw)
        p.wait("vector", "sync")
        nj = len(jlist)
        j0 = jlist[0]
        tt(yg[:, 0:nj, 0:cw], xn[:, j0:j0 + nj, c_a:c_b],
           jbc(rs_b[:, 0:cw], nj))
        tt(yg[:, 0:nj, 0:cw], yg[:, 0:nj, 0:cw], jbc(mu_b[:, 0:cw], nj),
           op=ALU.subtract)
        for idx, j in enumerate(jlist):
            ts2(out_tile(idx, j), yg[:, idx, 0:cw],
                lng_s[:, j:j + 1], lnb_s[:, j:j + 1])

    def mamba_chunk(l, ci):
        prof = (l == 1 and ci == 4)

        def sp(nm):
            if prof:
                p.phase(nm)
        a, b = CHUNKS[ci]
        cw = b - a
        swc = cw + 1
        w = cl[l]
        def gview(buf):
            return buf[:, 0:GN * swc].rearrange("p (g s) -> p g s", s=swc)

        daFq = [gview(dAq[0]), gview(dAq[1])]
        uuFq = [gview(uuq[0]), gview(uuq[1])]
        hbFq = [gview(hbq[0]), gview(hbq[1])]
        tmFq = [gview(tmq[0]), gview(tmq[1])]
        BbF = Bb[:, 0:NS * swc].rearrange("p (g s) -> p g s", s=swc)
        CbF = Cb[:, 0:NS * swc].rearrange("p (g s) -> p g s", s=swc)

        sp("m_rms")
        for j in range(4):
            tt(delta[:, j, 0:cw], h[:, j, a:b], h[:, j, a:b])
        p.wait("tensor", "vector")
        i = grab_bank()
        for j in range(4):
            p.emit("tensor", lambda e, j=j, i=i: e.matmul(
                pb[i][0:1, 0:cw], ones_b[:], delta[:, j, 0:cw],
                start=(j == 0), stop=(j == 3)), inc=(j == 3))
        p.wait("scalar", "tensor")
        rsqrt_row(pb[i][0:1, 0:cw], 1.0 / DC, 1, cw)
        free_bank(i, "scalar")
        pe_bcast_row(1, rs_b, cw)
        for j in range(4):
            stt(hn[:, j, 0:cw], h[:, j, a:b], w["rmsw"][:, j:j + 1],
                rs_b[:, 0:cw], op0=ALU.mult, op1=ALU.mult)

        sp("m_inproj")
        p.wait("tensor", "vector")
        for og in list(range(8, 16)) + list(range(8)):
            i = mm([(inWT_b[:, k, 128 * og:128 * (og + 1)], hn[:, k, 0:cw])
                    for k in range(4)], out_cols=cw)
            p.wait("scalar", "tensor")
            if og >= 8:
                act(sz[:, og - 8, 0:cw], pb[i][:, 0:cw], ACTF.Silu)
            else:
                acopy(xc_pre[:, og, 3:3 + cw], pb[i][:, 0:cw])
            free_bank(i, "scalar")
            if l == 0 and og % 2 == 0:
                emit_piece()

        if ci == 0:
            p.wait("vector", "scalar")
            tt(xc_pre[:, :, 3:3 + WU], xc_pre[:, :, 3:3 + WU],
               jbc(mask_b[:, 0:WU], 8))

        sp("m_conv")
        p.wait("vector", "scalar")
        for j in range(8):
            ts2(yg[:, j, 0:cw], xc_pre[:, j, 0:cw], w["convw"][:, j, 0:1],
                w["convb"][:, j:j + 1], op0=ALU.mult, op1=ALU.add)
            for k in range(1, 4):
                stt(yg[:, j, 0:cw], xc_pre[:, j, k:k + cw],
                    w["convw"][:, j, k:k + 1], yg[:, j, 0:cw])
        p.wait("scalar", "vector")
        acopy(xc_pre[:, :, 0:3], xc_pre[:, :, cw:cw + 3])
        act(xc_pre[:, :, 3:3 + cw], yg[:, :, 0:cw], ACTF.Silu)

        sp("m_xp")
        p.wait("tensor", "scalar")
        i = mm([(xpWT_b[:, j, :], xc[:, j, 0:cw]) for j in range(8)],
               out_rows=64, out_cols=cw)
        p.wait("scalar", "tensor")
        acopy(dtbc[0:64, 0:cw], pb[i][0:64, 0:cw])
        free_bank(i, "scalar")

        p.wait("sync", "scalar")
        dma(bcscr[:, 0:cw], bc_b[:, 0:cw])
        p.wait("sync", "sync")
        for g_ in range(NGRP):
            n0 = g_ * GN
            dma(BbF[:, n0:n0 + GN, 1:swc],
                bcscr[n0:n0 + GN, 0:cw].unsqueeze(0)
                .broadcast_to([128, GN, cw]))
            dma(CbF[:, n0:n0 + GN, 1:swc],
                bcscr[16 + n0:16 + n0 + GN, 0:cw].unsqueeze(0)
                .broadcast_to([128, GN, cw]))
        p.wait("sync", "sync")
        bcast_mark = p.mark("sync")

        sp("m_dt")
        p.wait("tensor", "scalar")
        for og in range(8):
            i = mm([(dtWT_b[0:DTR, 128 * og:128 * (og + 1)], dt_b[0:DTR, 0:cw])],
                   out_cols=cw)
            p.wait("scalar", "tensor")
            act(yg[:, og, 0:cw], pb[i][:, 0:cw], ACTF.Exp,
                bias=w["dtb"][:, og:og + 1])
            act(delta[:, og, 0:cw], yg[:, og, 0:cw], ACTF.Ln, bias=1.0)
            free_bank(i, "scalar")
            if l == 0:
                emit_piece()
        if ci == 0:
            p.wait("vector", "scalar")
            tt(delta[:, :, 0:WU], delta[:, :, 0:WU], jbc(mask_b[:, 0:WU], 8))

        p.wait("vector", "scalar")
        tt(wle[:, :, 0:cw], delta[:, :, 0:cw], xc[:, :, 0:cw])
        acopy(daFq[0][:, :, 0:1], zrow_b[:, 0:GN].unsqueeze(2))
        acopy(daFq[1][:, :, 0:1], zrow_b[:, 0:GN].unsqueeze(2))

        sp("m_scan")
        p.wait("scalar", "vector")
        scan_of = {}
        save_done = {}
        acc_last = None
        pend_save = None
        it = 0

        def flush_save():
            nonlocal pend_save
            if pend_save is not None:
                sl, hv, sm, qh = pend_save
                p.wait("scalar", "vector", sm)
                acopy(sl, hv)
                save_done[qh] = p.mark("scalar")
                pend_save = None

        for j in range(8):
            for g in range(NGRP):
                q = it % 2
                qh = it % 2
                daF, uuF = daFq[q], uuFq[q]
                hbF, tmF = hbFq[qh], tmFq[0]
                if q in scan_of:
                    p.wait("scalar", "vector", scan_of[q])
                for nl in range(GN):
                    n = g * GN + nl
                    act(daF[:, nl, 1:swc], delta[:, j, 0:cw], ACTF.Exp,
                        scale=w["As"][:, j, n:n + 1])
                acopy(uuF[:, :, 0:1], Sv[:, g * GN:(g + 1) * GN, j:j + 1])
                inj = p.mark("scalar")
                flush_save()
                if j == 0:
                    p.wait("vector", "sync", bcast_mark)
                tt(uuF[:, :, 1:swc], jbc(wle[:, j, 0:cw], GN),
                   BbF[:, g * GN:(g + 1) * GN, 1:swc])
                p.wait("vector", "scalar", inj)
                if qh in save_done:
                    p.wait("vector", "scalar", save_done[qh])
                p.emit("vector", lambda e, fl=GN * swc, q=q, qh=qh:
                       e.tensor_tensor_scan(
                           out=hbq[qh][:, 0:fl], data0=dAq[q][:, 0:fl],
                           data1=uuq[q][:, 0:fl], initial=0.0,
                           op0=ALU.mult, op1=ALU.add))
                scan_of[q] = p.mark("vector")
                if acc_last is not None:
                    p.wait("vector", "tensor", acc_last)
                tt(tmF[:, :, 1:swc], hbF[:, :, 1:swc],
                   CbF[:, g * GN:(g + 1) * GN, 1:swc])
                ym = p.mark("vector")
                pend_save = (Sv[:, g * GN:(g + 1) * GN, j:j + 1],
                             hbF[:, :, swc - 1:swc], scan_of[q], qh)
                p.wait("tensor", "vector", ym)
                if g == 0 and acc_free[j % 2] is not None:
                    p.wait("tensor", acc_free[j % 2][0], acc_free[j % 2][1])
                for nl in range(GN):
                    p.emit("tensor", lambda e, nl=nl, j=j, g=g, swc=swc, cw=cw,
                           tm3=tmF: e.matmul(
                               pa[j % 2][:, 0:cw], eye_b[:], tm3[:, nl, 1:swc],
                               start=(g == 0 and nl == 0),
                               stop=(g == NGRP - 1 and nl == GN - 1)),
                           inc=(nl == GN - 1))
                acc_last = p.mark("tensor")
                it += 1
            p.wait("vector", "tensor", acc_last)
            stt(yg[:, j, 0:cw], xc[:, j, 0:cw], w["Dp"][:, j:j + 1],
                pa[j % 2][:, 0:cw])
            tt(yg[:, j, 0:cw], yg[:, j, 0:cw], sz[:, j, 0:cw])
            acc_free[j % 2] = ("vector", p.mark("vector"))
            if l == 0:
                emit_piece()
                emit_piece()
        flush_save()

        sp("m_mo")
        p.wait("tensor", "vector")
        for og in range(4):
            i = mm([(moWT_b[:, k, 128 * og:128 * (og + 1)], yg[:, k, 0:cw])
                    for k in range(8)], out_cols=cw)
            p.wait("vector", "tensor")
            tt(h[:, og, a:b], h[:, og, a:b], pb[i][:, 0:cw], op=ALU.add)
            free_bank(i, "vector")

    p.barrier()
    memset(S[:], 0.0)
    memset(xc_pre[:, :, 0:3], 0.0)
    for ci in range(len(CHUNKS)):
        p.phase(f"L0c{ci}")
        p.barrier()
        a, b = CHUNKS[ci]
        ln2_affine(a, b, [0, 1, 2, 3],
                   lambda idx, j, a=a, b=b: h[:, j, a:b])
        mamba_chunk(0, ci)

    while piece_state["fin"] < len(pieces):
        emit_piece()
    ffn_scratch_done = p.mark("sync")

    p.barrier()
    load_layer_weights(1)
    memset(S[:], 0.0)
    memset(xc_pre[:, :, 0:3], 0.0)
    for ci in range(len(CHUNKS)):
        p.phase(f"L1c{ci}")
        p.barrier()
        mamba_chunk(1, ci)
        if ci == 4:
            p.phase("m_end")

    p.barrier()
    p.wait("sync", "sync")
    for k in range(8):
        for hf in range(2):
            loadcast(poWT[k][:, 512 * hf:512 * (hf + 1)], stg_f[:, 0:512],
                     poWT_b[:, k, 512 * hf:512 * (hf + 1)])
    for ci in range(1, len(CHUNKS)):
        p.phase(f"P3c{ci}")
        p.barrier()
        a, b = CHUNKS[ci]
        cw = b - a
        oa = a - WU
        ln2_affine(a, b, list(range(8)),
                   lambda idx, j, cw=cw: xc[:, j, 0:cw])
        p.wait("vector", "vector")
        tt(hn[:, 0:4, 0:cw], xc[:, 4:8, 0:cw], h[:, :, a:b], op=ALU.add)
        p.wait("tensor", "vector")
        for og in range(8):
            pairs = [(poWT_b[:, k, 128 * og:128 * (og + 1)], xc[:, k, 0:cw])
                     for k in range(4)]
            pairs += [(poWT_b[:, k + 4, 128 * og:128 * (og + 1)],
                       hn[:, k, 0:cw]) for k in range(4)]
            i = mm(pairs, out_cols=cw)
            p.wait("vector", "tensor")
            stt(sz[:, og, 0:cw], pb[i][:, 0:cw], pob_s[:, og:og + 1],
                xc[:, og, 0:cw], op0=ALU.add, op1=ALU.add)
            free_bank(i, "vector")
        gu_sm = {}
        gu_mm = {}

        def gu_fetch(og):
            q = og % 2
            if q in gu_mm:
                p.wait("sync", "tensor", gu_mm[q])
            dma(wgt_q[q], wg_s[og])
            dma(wut_q[q], wu_s[og])
            gu_sm[q] = p.mark("sync")

        gu_fetch(0)
        for og in range(24):
            q = og % 2
            p.wait("tensor", "sync", gu_sm[q])
            ig = mm([(wgt_q[q][:, k, :], xn[:, k, a:b]) for k in range(8)],
                    out_cols=cw)
            iu = mm([(wut_q[q][:, k, :], xn[:, k, a:b]) for k in range(8)],
                    out_cols=cw)
            gu_mm[q] = p.mark("tensor")
            if og + 1 < 24:
                gu_fetch(og + 1)
            p.wait("scalar", "tensor")
            p.wait("scalar", "vector")
            act(sg_s[:, 0:cw], pb[ig][:, 0:cw], ACTF.Silu, scale=QINV)
            free_bank(ig, "scalar")
            p.wait("vector", "scalar")
            stt(hup[:, og, 0:cw], sg_s[:, 0:cw], QINV, pb[iu][:, 0:cw],
                op0=ALU.mult, op1=ALU.mult)
            free_bank(iu, "vector")
        dn_sm = {}
        dn_mm = {}

        def dn_fetch(oc):
            q = oc % 2
            if q in dn_mm:
                p.wait("sync", "tensor", dn_mm[q])
            dma(wdt_q[q], wd_s[oc])
            dn_sm[q] = p.mark("sync")

        dn_fetch(0)
        for oc in range(8):
            q = oc % 2
            p.wait("tensor", "sync", dn_sm[q])
            p.wait("tensor", "vector")
            i = mm([(wdt_q[q][:, kk, :], hup[:, kk, 0:cw]) for kk in range(24)],
                   out_cols=cw)
            dn_mm[q] = p.mark("tensor")
            if oc + 1 < 8:
                dn_fetch(oc + 1)
            p.wait("scalar", "tensor")
            act(wle[:, oc, 0:cw], pb[i][:, 0:cw], ACTF.Silu, scale=QINV)
            free_bank(i, "scalar")
        p.wait("vector", "scalar")
        tt(sz[:, :, 0:cw], sz[:, :, 0:cw], wle[:, :, 0:cw])
        for sub in range(cw // CWF):
            sa = a + CWF * sub
            p.wait("sync", "vector")
            p.wait("sync", "tensor")
            dma(x_ch[:, :, 0:CWF],
                xT.rearrange("(j q) t -> q j t", q=128)[:, :, sa:sa + CWF])
            p.wait("vector", "sync")
            tt(x_ch[:, :, 0:CWF], x_ch[:, :, 0:CWF],
               sz[:, :, CWF * sub:CWF * (sub + 1)], op=ALU.add)
            p.wait("sync", "vector")
            dma(outT.rearrange("(j q) t -> q j t", q=128)
                [:, :, oa + CWF * sub:oa + CWF * (sub + 1)],
                x_ch[:, :, 0:CWF])

    p.barrier()
    for e in ENGS:
        p.wait(e, e, p.mark(e))

    for _sn in p.cum:
        sems[_sn] = es.enter_context(nc.semaphore(f"sem_{_sn}"))
    block = es.enter_context(nc.Block())

    strict_edges = bool(os.environ.get("BASS_SIM_FP64"))

    def run_queue(engobj, name):
        done = 0
        bounds = [(nm, idx[name]) for (nm, idx) in p.phases]
        bounds.append(("end", len(p.q[name])))
        qi = 0
        for bi in range(len(bounds) - 1):
            nm, lo = bounds[bi]
            hi = bounds[bi + 1][1]
            if hi <= lo:
                continue
            with nc.named_scope(nm):
                for item in p.q[name][lo:hi]:
                    if item[0] == "WAIT":
                        engobj.wait_ge(sems[item[1]], item[2])
                    else:
                        if strict_edges and done > 0:
                            engobj.wait_ge(sems[name], done)
                        instr = item[1](engobj)
                        if item[2]:
                            instr.then_inc(sems[name], item[3])
                            done += item[3]
        assert qi == 0

    @block.sync
    def _(e):
        run_queue(e, "sync")

    @block.tensor
    def _(e):
        run_queue(e, "tensor")

    @block.scalar
    def _(e):
        run_queue(e, "scalar")

    @block.vector
    def _(e):
        run_queue(e, "vector")

    @block.gpsimd
    def _(e):
        run_queue(e, "gpsimd")

    es.close()
    return nc


_NC_CACHE = [None]


def kernel(L0_attn_src, L0_attn_ln_g, L0_attn_ln_b, L0_attn_po_W, L0_attn_po_b,
           L0_attn_rms_w, L0_attn_in_W, L0_attn_conv_w, L0_attn_conv_b,
           L0_attn_xp_W, L0_attn_dt_W, L0_attn_dt_b, L0_attn_A_log, L0_attn_Dp,
           L0_attn_mo_W, L0_ffn_w_gate, L0_ffn_w_up, L0_ffn_w_down):
    def f(x):
        return np.ascontiguousarray(np.asarray(x), dtype=np.float32)

    def ii(x):
        return np.ascontiguousarray(np.asarray(x), dtype=np.int32)

    def col128(v):
        return np.ascontiguousarray(f(v).reshape(8, 128).T)

    x = f(L0_attn_src)
    common = dict(
        cvals=np.tile(np.array([0.0, 1.0, -0.5, -1.0, EPS, QINV,
                                1.0 / 512, 1.0 / 1024], np.float32), (128, 1)),
        eye_in=np.eye(128, dtype=np.float32),
        ln_g=col128(L0_attn_ln_g),
        ln_b=col128(L0_attn_ln_b),
        po_bi=col128(L0_attn_po_b),
        poWT=np.ascontiguousarray(f(L0_attn_po_W).T.reshape(8, 128, D)),
        wgT=np.ascontiguousarray(ii(L0_ffn_w_gate).T.reshape(8, 128, FH)),
        wuT=np.ascontiguousarray(ii(L0_ffn_w_up).T.reshape(8, 128, FH)),
        wdT=np.ascontiguousarray(ii(L0_ffn_w_down).T.reshape(24, 128, D)),
    )
    for l in range(2):
        common[f"inWT{l}"] = np.ascontiguousarray(
            f(L0_attn_in_W[l]).T.reshape(4, 128, 2 * DI))
        common[f"xpWT{l}"] = np.ascontiguousarray(
            f(L0_attn_xp_W[l]).T.reshape(8, 128, 64))
        common[f"dtWT{l}"] = np.ascontiguousarray(f(L0_attn_dt_W[l]).T)
        common[f"moWT{l}"] = np.ascontiguousarray(
            f(L0_attn_mo_W[l]).T.reshape(8, 128, DC))
        common[f"convw{l}"] = np.ascontiguousarray(
            f(L0_attn_conv_w[l][:, 0, :]).reshape(8, 128, 4).transpose(1, 0, 2))
        common[f"convb{l}"] = col128(L0_attn_conv_b[l])
        common[f"dtb{l}"] = col128(L0_attn_dt_b[l])
        common[f"Alog{l}"] = np.ascontiguousarray(
            f(L0_attn_A_log[l]).reshape(8, 128, NS).transpose(1, 0, 2))
        common[f"Dp{l}"] = col128(L0_attn_Dp[l])
        common[f"rmsw{l}"] = np.ascontiguousarray(
            f(L0_attn_rms_w[l]).reshape(4, 128).T)

    in_maps = []
    for core in range(8):
        bidx, s = divmod(core, NSLICE)
        t0 = s * TO
        if s == 0:
            blk = np.concatenate(
                [np.zeros((WU, D), np.float32), x[bidx, 0:TO]], axis=0)
            mask = np.zeros((128, WU), np.float32)
        else:
            blk = x[bidx, t0 - WU:t0 + TO]
            mask = np.ones((128, WU), np.float32)
        m = dict(common)
        m["xT"] = np.ascontiguousarray(blk.T)
        m["mask2d"] = mask
        in_maps.append(m)

    if _NC_CACHE[0] is None:
        _NC_CACHE[0] = build_nc()
    res = run_bass_kernel_spmd(_NC_CACHE[0], in_maps, core_ids=list(range(8)))

    out = np.empty((B, L, D), np.float32)
    for core in range(8):
        bidx, s = divmod(core, NSLICE)
        out[bidx, s * TO:(s + 1) * TO] = res.results[core]["outT"].T
    return out
